# version 02
# baseline (speedup 1.0000x reference)
import numpy as np
import ml_dtypes
import concourse.bass as bass
import concourse.mybir as mybir
from concourse.bass_core_rust import IndirectOffsetOnAxis
from concourse.bass_utils import run_bass_kernel_spmd

F32 = mybir.dt.float32
BF16 = mybir.dt.bfloat16
I32 = mybir.dt.int32
U8 = mybir.dt.uint8
AF = mybir.ActivationFunctionType
ALU = mybir.AluOpType
AX = mybir.AxisListType
BF16NP = ml_dtypes.bfloat16

B, T, C, H, HD, FF = 4, 2048, 4096, 32, 128, 11008
NCC = C // 128
NHB = FF // 128
NDB = C // 128
LCS = [3, 3, 3, 3, 4, 4, 4, 4]
SCALE_S = float(1.0 / np.sqrt(128.0))
SCALE_A = 1.0 / 64.0
EPS = 1e-5
NEG = -1.0e30

_DEBUG_TAPS = ()
_CLEANUP = 2
_FIN_TAIL = 0

ENGINES = ("sync", "gpsimd", "vector", "scalar", "tensor")
SEM_OF = {"sync": "sd", "gpsimd": "gp", "vector": "dv", "scalar": "se", "tensor": "pe"}


def _flat(ws):
    out = []
    for w in ws:
        if w is None:
            continue
        if isinstance(w, list):
            out.extend(_flat(w))
        else:
            out.append(w)
    return out


class Prog:
    def __init__(self):
        self.ops = {e: [] for e in ENGINES}
        self.val = {}

    def emit(self, eng, fn, wait=(), inc=None):
        w = []
        for (sn, v) in _flat(list(wait)):
            if v > 0:
                w.append((sn, v))
        tok = None
        if inc is not None:
            sn, amt = inc
            self.val[sn] = self.val.get(sn, 0) + amt
            tok = (sn, self.val[sn])
        self.ops[eng].append((fn, tuple(w), (inc[0], inc[1]) if inc else None))
        return tok

    def barrier(self):
        return [(s, v) for s, v in sorted(self.val.items())]

    def run(self, block, SH):
        for ename in ENGINES:
            items = self.ops[ename]

            def mk(items):
                def body(eng):
                    seen = {}
                    for fn, waits, inc in items:
                        for (sn, v) in waits:
                            if seen.get(sn, 0) >= v:
                                continue
                            eng.wait_ge(SH[sn], v)
                            seen[sn] = v
                        ins = fn(eng)
                        if inc is not None:
                            ins.then_inc(SH[inc[0]], inc[1])
                return body

            getattr(block, ename)(mk(items))


def _build():
    nc = bass.Bass("TRN2", target_bir_lowering=False)
    dt_ = nc.dram_tensor

    d_x = dt_("x_seq", [T, C], F32, kind="ExternalInput").ap()
    d_xT = dt_("xT_own", [NDB, 128, 1024], F32, kind="ExternalInput").ap()
    d_masks = dt_("masks", [8, 4, 128, 512], BF16, kind="ExternalInput").ap()
    d_cosq = dt_("cos_q", [128, 1024], F32, kind="ExternalInput").ap()
    d_sinq = dt_("sin_q", [128, 1024], F32, kind="ExternalInput").ap()
    d_cosf = dt_("cos_f", [128, T], F32, kind="ExternalInput").ap()
    d_sinf = dt_("sin_f", [128, T], F32, kind="ExternalInput").ap()
    d_sel = dt_("sel", [128, 2], F32, kind="ExternalInput").ap()
    d_gate = dt_("gating", [1, 1], F32, kind="ExternalInput").ap()
    d_Aat = dt_("A_attn", [96, 128, C], F32, kind="ExternalInput").ap()
    d_Apr = dt_("A_proj", [NDB, 128, C], F32, kind="ExternalInput").ap()
    d_Aad = dt_("A_ad", [128, 320], F32, kind="ExternalInput").ap()
    d_Bg = dt_("B_g", [NHB, 128, C], I32, kind="ExternalInput").ap()
    d_Bu = dt_("B_u", [NHB, 128, C], I32, kind="ExternalInput").ap()
    d_Bd = dt_("B_d", [NDB, 128, FF], I32, kind="ExternalInput").ap()
    d_sgt = dt_("s_g_t", [128, 2752], F32, kind="ExternalInput").ap()
    d_sut = dt_("s_u_t", [128, 2752], F32, kind="ExternalInput").ap()
    d_sdt = dt_("s_d_t", [128, 2752], F32, kind="ExternalInput").ap()
    d_sgn = dt_("s_g_n", [128, 2752], F32, kind="ExternalInput").ap()
    d_sun = dt_("s_u_n", [128, 2752], F32, kind="ExternalInput").ap()
    d_sdn = dt_("s_d_n", [128, 2752], F32, kind="ExternalInput").ap()
    d_zgn = dt_("z_g_n", [128, 2752], I32, kind="ExternalInput").ap()
    d_zun = dt_("z_u_n", [128, 2752], I32, kind="ExternalInput").ap()
    d_zdn = dt_("z_d_n", [128, 2752], I32, kind="ExternalInput").ap()

    scratch_reg = {}

    def scr(name, shape, dtype):
        ap = dt_(name, shape, dtype, kind="Internal").ap()
        scratch_reg[name] = (ap, shape)
        return ap

    d_xn = scr("xn_scr", [T, C], BF16)
    d_ks = scr("k_scr", [H, 128, T], BF16)
    d_vs = scr("v_scr", [H, 128, T], BF16)
    d_qs = scr("q_scr", [H, 128, 1024], BF16)
    d_hs = scr("h_scr", [NHB, 128, 1024], BF16)
    d_fs = scr("f_scr", [NDB, 128, 1024], F32)
    d_sgb = scr("sgb_scr", [NHB, C], BF16)
    d_sub = scr("sub_scr", [NHB, C], BF16)
    d_sdb = scr("sdb_scr", [NDB, FF], BF16)
    d_mtg = scr("mtg_scr", [32, FF], BF16)
    d_mtu = scr("mtu_scr", [32, FF], BF16)
    d_mtd = scr("mtd_scr", [NHB, C], BF16)
    d_avs = scr("av_scr", [10, C], BF16)

    d_out = dt_("out_fm", [NDB, 128, 1024], F32, kind="ExternalOutput").ap()

    P = Prog()
    from contextlib import ExitStack
    with ExitStack() as st:
        en = st.enter_context

        RA = en(nc.sbuf_tensor("RA", [128, 96 * 1024], U8))
        RB = en(nc.sbuf_tensor("RB", [128, 64 * 1024], U8))
        AR = en(nc.sbuf_tensor("AR", [128, 24 * 1024], U8))

        def carve(region, off, dtype, shape):
            isz = {F32: 4, BF16: 2, I32: 4}[dtype]
            n = int(np.prod(shape[1:])) * isz
            assert off + n <= region.shape[1], (off, n, shape)
            ap = region[0:shape[0], off:off + n].bitcast(dtype)
            if len(shape) == 3:
                ap = ap.rearrange("p (a b) -> p a b", a=shape[1], b=shape[2])
            return ap

        v_xnfm = carve(RA, 0, BF16, [128, NCC, 1024])
        v_xtm = [carve(RA, 64 * 1024 + i * 16 * 1024, F32, [128, C]) for i in range(2)]
        v_slab = [carve(RA, 64 * 1024 + i * 8 * 1024, BF16, [128, C]) for i in range(2)]
        v_xnst = [carve(RB, i * 8 * 1024, BF16, [128, C]) for i in range(2)]
        v_gath = [carve(RA, i * 8 * 1024, BF16, [128, C]) for i in range(2)]
        v_gA = [carve(RA, 16 * 1024 + i * 8 * 1024, BF16, [128, C]) for i in range(2)]
        v_xno = carve(RB, 0, BF16, [128, NCC, 1024])
        v_wg = [carve(RA, i * 8 * 1024, BF16, [128, C]) for i in range(2)]
        v_wu = [carve(RA, 16 * 1024 + i * 8 * 1024, BF16, [128, C]) for i in range(2)]
        v_sg = [carve(RA, 32 * 1024 + i * 8 * 1024, BF16, [128, C]) for i in range(2)]
        v_su = [carve(RA, 48 * 1024 + i * 8 * 1024, BF16, [128, C]) for i in range(2)]
        v_hst = [carve(RA, 64 * 1024 + i * 2 * 1024, BF16, [128, 1024]) for i in range(2)]
        v_gsil = [carve(RA, 68 * 1024 + i * 2 * 1024, F32, [128, 512]) for i in range(2)]
        v_hhalf = carve(RA, 0, BF16, [128, NHB, 512])
        v_wd = [carve(RB, i * 22 * 1024, BF16, [128, NHB, 128]) for i in range(2)]
        v_sdq = carve(RB, 44 * 1024, BF16, [128, 5504])
        v_fst = [carve(RB, 56 * 1024 + i * 2 * 1024, F32, [128, 512]) for i in range(2)]
        v_y = carve(RB, 0, BF16, [128, H, 1024])
        v_mask = carve(RA, 0, BF16, [128, 32, 512])
        v_kh = [carve(RA, 32 * 1024 + i * 4 * 1024, BF16, [128, T]) for i in range(2)]
        v_vh = [carve(RA, 40 * 1024 + i * 4 * 1024, BF16, [128, T]) for i in range(2)]
        v_qh = [carve(RA, 48 * 1024 + i * 2 * 1024, BF16, [128, 1024]) for i in range(2)]
        v_vtm = carve(RA, 52 * 1024, BF16, [128, 16, 128])
        v_Ssb = carve(RA, 56 * 1024, F32, [128, T])
        v_Pb = carve(RA, 64 * 1024, BF16, [128, T])
        v_PT = carve(RA, 68 * 1024, BF16, [128, 16, 512])
        v_asc = carve(RA, 84 * 1024, F32, [128, 16])
        v_Pa = carve(RA, 85 * 1024, BF16, [128, 16])
        v_ft = [carve(RA, i * 4 * 1024, F32, [128, 1024]) for i in range(2)]
        v_xt = [carve(RA, 8 * 1024 + i * 4 * 1024, F32, [128, 1024]) for i in range(2)]
        v_ost = [carve(RA, 16 * 1024 + i * 4 * 1024, F32, [128, 1024]) for i in range(2)]
        v_cosf = carve(AR, 0, F32, [128, T])
        v_sinf = carve(AR, 8 * 1024, F32, [128, T])
        v_cosq = carve(AR, 16 * 1024, F32, [128, 1024])
        v_sinq = carve(AR, 20 * 1024, F32, [128, 1024])
        v_i_s = carve(AR, 0, F32, [128, 2752])
        v_i_z = carve(AR, 11 * 1024 + 256, F32, [128, 2752])
        v_i_mb = carve(AR, 0, BF16, [128, 2752])
        v_mtd_t = [carve(AR, 6 * 1024 + i * 256, BF16, [86, 128]) for i in range(2)]
        v_Xh = carve(AR, 7 * 1024, BF16, [86, 512])

        sb_id = en(nc.sbuf_tensor("sb_id", [128, 128], BF16))
        sb_p64 = en(nc.sbuf_tensor("sb_p64", [128, 128], BF16))
        sb_e1 = en(nc.sbuf_tensor("sb_e1", [128, 128], F32))
        sb_st = en(nc.sbuf_tensor("sb_st", [128, 171], BF16))
        sb_i32a = en(nc.sbuf_tensor("sb_i32a", [128, 171], I32))
        sb_i32b = en(nc.sbuf_tensor("sb_i32b", [128, 128], I32))
        sb_ad = en(nc.sbuf_tensor("sb_ad", [128, 320], BF16))
        sb_ak = en(nc.sbuf_tensor("sb_ak", [128, 320], BF16))
        sb_sel = en(nc.sbuf_tensor("sb_sel", [128, 2], F32))
        sb_gat = en(nc.sbuf_tensor("sb_gat", [128, 1], F32))
        sb_cols = en(nc.sbuf_tensor("sb_cols", [128, 64], F32))
        sb_kst = en(nc.sbuf_tensor("sb_kst", [128, 2048], BF16))
        sb_vs = en(nc.sbuf_tensor("sb_vs", [128, 512], BF16))
        sb_tc = en(nc.sbuf_tensor("sb_tc", [128, 512], F32))
        sb_ave = en(nc.sbuf_tensor("sb_ave", [128, 16], BF16))
        sb_avr = en(nc.sbuf_tensor("sb_avr", [10, 128], BF16))
        sb_avt = en(nc.sbuf_tensor("sb_avt", [10, 256], BF16))
        sb_pan = en(nc.sbuf_tensor("sb_pan", [10, 512], BF16))
        sb_mtg = en(nc.sbuf_tensor("sb_mtg", [32, 256], BF16))
        sb_mtu = en(nc.sbuf_tensor("sb_mtu", [32, 256], BF16))
        sb_Xd = en(nc.sbuf_tensor("sb_Xd", [32, 1024], BF16))

        ps = [en(nc.psum_tensor(f"ps{i}", [128, 512], F32)) for i in range(6)]
        pstp = [en(nc.psum_tensor(f"pstp{i}", [128, 1024], BF16)) for i in range(2)]

        block = en(nc.Block())
        E = P.emit

        t_eps = E("vector", lambda v: v.memset(sb_cols[:, 63:64], EPS), inc=("dv", 1))
        t_id = E("gpsimd", lambda g: g.iota(sb_i32b[:], [[-1, 128]], base=0, channel_multiplier=1),
                 inc=("gpc", 1))
        t_id2 = E("vector", lambda v: v.tensor_single_scalar(sb_id[:], sb_i32b[:], 0, ALU.is_equal),
                  wait=[t_id], inc=("dv", 1))
        t_p1 = E("vector", lambda v: v.tensor_single_scalar(sb_p64[:], sb_i32b[:], -64, ALU.is_equal),
                 wait=[t_id], inc=("dv", 1))
        t_p2 = E("vector", lambda v: v.tensor_single_scalar(sb_e1[:], sb_i32b[:], 64, ALU.is_equal),
                 wait=[t_id], inc=("dv", 1))
        t_p3 = E("vector", lambda v: v.tensor_sub(sb_p64[:], sb_p64[:], sb_e1[:]),
                 wait=[t_p1, t_p2], inc=("dv", 1))
        t_st0 = E("gpsimd", lambda g: g.iota(sb_i32a[:], [[1, 171]], base=-85, channel_multiplier=0),
                  inc=("gpc", 1))
        t_st = E("vector", lambda v: v.tensor_single_scalar(sb_st[:], sb_i32a[:], 0, ALU.is_equal),
                 wait=[t_st0], inc=("dv", 1))
        t_ad = E("gpsimd", lambda g: g.dma_start(out=sb_ad[:], in_=d_Aad), inc=("gad", 16))
        t_gx = E("sync", lambda g: g.dma_start(out=sb_sel[:], in_=d_sel), inc=("sgx", 16))
        t_gt = E("gpsimd", lambda g: g.dma_start(out=sb_gat[:], in_=d_gate.partition_broadcast(128)),
                 inc=("ggt", 16))

        prev_store = None
        last_tm = None
        for (dsn, dst_b, dzn, dmt) in ((d_sgt, d_sgb, None, None),
                                       (d_sut, d_sub, None, None),
                                       (d_sdt, d_sdb, None, None),
                                       (d_sgn, None, d_zgn, d_mtg),
                                       (d_sun, None, d_zun, d_mtu),
                                       (d_sdn, None, d_zdn, d_mtd)):
            tl = E("sync", lambda g, dsn=dsn: g.dma_start(out=v_i_s[:], in_=dsn),
                   wait=[prev_store, t_p3, last_tm], inc=("sil", 16))
            if dst_b is not None:
                dv = dst_b.rearrange("a b -> (a b)").rearrange("(p f) -> p f", p=128)
                prev_store = E("gpsimd", lambda g, dv=dv: g.dma_start(out=dv, in_=v_i_s[:]),
                               wait=[tl], inc=("gist", 16))
            else:
                tz = E("gpsimd", lambda g, dzn=dzn: g.dma_start(out=v_i_z[:], in_=dzn),
                       wait=[tl], inc=("gizl", 16))
                tm = E("vector", lambda v: v.tensor_mul(v_i_z[:], v_i_z[:], v_i_s[:]),
                       wait=[tz], inc=("dv", 1))
                last_tm = E("vector", lambda v: v.tensor_scalar(v_i_mb[:], v_i_z[:], -1.0, None,
                                                                ALU.mult),
                            wait=[tm], inc=("dv", 1))
                dv = dmt.rearrange("a b -> (a b)").rearrange("(p f) -> p f", p=128)
                prev_store = E("sync", lambda g, dv=dv: g.dma_start(out=dv, in_=v_i_mb[:]),
                               wait=[last_tm], inc=("sist", 16))
        t_cf = E("sync", lambda g: g.dma_start(out=v_cosf[:], in_=d_cosf),
                 wait=[prev_store, last_tm], inc=("scos", 16))
        t_sf = E("sync", lambda g: g.dma_start(out=v_sinf[:], in_=d_sinf), inc=("scos", 16))
        t_cq = E("sync", lambda g: g.dma_start(out=v_cosq[:], in_=d_cosq), inc=("scos", 16))
        t_sq = E("sync", lambda g: g.dma_start(out=v_sinq[:], in_=d_sinq), inc=("scos", 16))
        t_cos = ("scos", 64)
        bar = P.barrier()

        ktok = {}
        slab_rd = {}
        ps_free = {}
        pstp_free = {}
        for hf in (0, 1):
            xnfm_ready = []
            for k8 in range(8):
                tau = hf * 8 + k8
                par = k8 % 2
                w = [bar] if (hf == 0 and k8 == 0) else []
                if hf == 1 and k8 < 2:
                    w = [slab_rd.get(("B", 0)), slab_rd.get(("B", 1))]
                tx = E("sync", lambda g, tau=tau, par=par: g.dma_start(
                    out=v_xtm[par][:], in_=d_x[tau * 128:(tau + 1) * 128, :]),
                    wait=w + [ktok.get(("xtm", par))], inc=("sx%d" % par, 16))
                tbn = None
                for ch in range(8):
                    tbn = E("vector", lambda v, ch=ch, par=par: v.bn_stats(
                        sb_cols[:, ch * 6:(ch + 1) * 6], v_xtm[par][:, ch * 512:(ch + 1) * 512]),
                        wait=[tx, ktok.get("ln_prev")], inc=("dv", 1))
                tag = E("vector", lambda v: v.bn_aggr(sb_cols[:, 48:50], sb_cols[:, 0:48]),
                        wait=[tbn], inc=("dv", 1))
                tcp = E("scalar", lambda s: s.copy(sb_cols[:, 52:54], sb_cols[:, 48:50]),
                        wait=[tag], inc=("se", 1))
                tsd = E("scalar", lambda s: s.activation(sb_cols[:, 50:51], sb_cols[:, 53:54],
                                                         AF.Sqrt, bias=sb_cols[:, 63:64]),
                        wait=[t_eps, tcp], inc=("se", 1))
                trc = E("vector", lambda v: v.reciprocal(sb_cols[:, 51:52], sb_cols[:, 50:51]),
                        wait=[tsd], inc=("dv", 1))
                tts = E("vector", lambda v, par=par: v.tensor_scalar(
                    v_xnst[par][:], v_xtm[par][:], sb_cols[:, 52:53], sb_cols[:, 51:52],
                    ALU.subtract, ALU.mult),
                    wait=[trc, ktok.get(("xnst_w", par)), ktok.get(("xnst_tp", par))],
                    inc=("dv", 1))
                ktok[("xtm", par)] = tts
                ktok["ln_prev"] = tts
                tw = E("sync", lambda g, tau=tau, par=par: g.dma_start(
                    out=d_xn[tau * 128:(tau + 1) * 128, :], in_=v_xnst[par][:]),
                    wait=[tts], inc=("sxw%d" % par, 16))
                ktok[("xnst_w", par)] = tw
                ttp_last = None
                for pk in range(4):
                    pb = pk % 2
                    tps = None
                    for j in range(8):
                        cc = pk * 8 + j
                        tps = E("tensor", lambda t, cc=cc, pb=pb, j=j, par=par: t.transpose(
                            pstp[pb][:, j * 128:(j + 1) * 128],
                            v_xnst[par][:, cc * 128:(cc + 1) * 128], sb_id[:]),
                            wait=([tts, t_id2, pstp_free.get(pb)] if j == 0 else []),
                            inc=(("pe", 1) if j == 7 else None))
                    tev = E("vector", lambda v, pk=pk, pb=pb, k8=k8: v.tensor_copy(
                        v_xnfm[:, pk * 8:(pk + 1) * 8, k8 * 128:(k8 + 1) * 128],
                        pstp[pb][:].rearrange("p (a b) -> p a b", a=8, b=128)),
                        wait=[tps], inc=("dv", 1))
                    pstp_free[pb] = tev
                    ttp_last = tev
                ktok[("xnst_tp", par)] = ttp_last
                xnfm_ready.append(ttp_last)
            xn_gate = list(xnfm_ready)

            for obi in range(64):
                ob = 32 + obi
                h = (ob - 32) % 32
                is_k = ob < 64
                par = obi % 2
                tsl = E("gpsimd", lambda g, ob=ob, par=par: g.dma_start(
                    out=v_slab[par][:], in_=d_Aat[ob]),
                    wait=[slab_rd.get(("B", par))] + (xn_gate if obi < 2 else []),
                    inc=("gsl%d" % par, 16))
                for g2 in range(2):
                    gg = hf * 2 + g2
                    bank = (obi * 2 + g2) % 4
                    tmm = None
                    for cc in range(NCC):
                        tmm = E("tensor", lambda t, cc=cc, g2=g2, par=par, bank=bank: t.matmul(
                            ps[bank][:], v_slab[par][:, cc * 128:(cc + 1) * 128],
                            v_xnfm[:, cc, g2 * 512:(g2 + 1) * 512],
                            start=(cc == 0), stop=(cc == NCC - 1)),
                            wait=([tsl, ps_free.get(bank)] + (xn_gate if obi == 0 else [])
                                  if cc == 0 else []),
                            inc=(("pe", 1) if cc == NCC - 1 else None))
                    stg = (obi * 2 + g2) % 4
                    gcol = slice(gg * 512, (gg + 1) * 512)
                    if is_k:
                        tvs = E("vector", lambda v, bank=bank, gcol=gcol: v.tensor_mul(
                            sb_vs[:], ps[bank][:], v_sinf[:, gcol]),
                            wait=[tmm, t_cos, ktok.get("rope_mm")], inc=("dv", 1))
                        trm = E("tensor", lambda t: t.matmul(
                            ps[4][:], sb_p64[:], sb_vs[:], start=True, stop=True),
                            wait=[tvs, t_p3, ktok.get("rope_ev")], inc=("pe", 1))
                        ktok["rope_mm"] = trm
                        ttc = E("vector", lambda v, bank=bank, gcol=gcol: v.tensor_mul(
                            sb_tc[:], ps[bank][:], v_cosf[:, gcol]),
                            wait=[tmm, ktok.get("rope_ev")], inc=("dv", 1))
                        ps_free[bank] = ttc
                        tkr = E("vector", lambda v, stg=stg: v.tensor_add(
                            sb_kst[:, stg * 512:(stg + 1) * 512], sb_tc[:], ps[4][:]),
                            wait=[trm, ttc, ktok.get(("kst", stg))], inc=("dv", 1))
                        ktok["rope_ev"] = tkr
                        tst_w = E("sync", lambda g, h=h, gcol=gcol, stg=stg: g.dma_start(
                            out=d_ks[h][:, gcol], in_=sb_kst[:, stg * 512:(stg + 1) * 512]),
                            wait=[tkr], inc=("skst%d" % stg, 16))
                        ktok[("kst", stg)] = tst_w
                    else:
                        tvv = E("vector", lambda v, bank=bank, stg=stg: v.tensor_copy(
                            sb_kst[:, stg * 512:(stg + 1) * 512], ps[bank][:]),
                            wait=[tmm, ktok.get(("kst", stg))], inc=("dv", 1))
                        ps_free[bank] = tvv
                        tst_w = E("sync", lambda g, h=h, gcol=gcol, stg=stg: g.dma_start(
                            out=d_vs[h][:, gcol], in_=sb_kst[:, stg * 512:(stg + 1) * 512]),
                            wait=[tvv], inc=("skst%d" % stg, 16))
                        ktok[("kst", stg)] = tst_w
                slab_rd[("B", par)] = tmm
                xn_gate = []
                if hf == 0:
                    tad = None
                    for cc in range(NCC):
                        tad = E("tensor", lambda t, cc=cc, par=par: t.matmul(
                            ps[5][:, 0:10], v_slab[par][:, cc * 128:(cc + 1) * 128],
                            sb_ad[:, cc * 10:(cc + 1) * 10],
                            start=(cc == 0), stop=(cc == NCC - 1)),
                            wait=([t_ad, ktok.get("ad_ev")] if cc == 0 else []),
                            inc=(("pe", 1) if cc == NCC - 1 else None))
                    slab_rd[("B", par)] = tad
                    if is_k:
                        tae = E("vector", lambda v, h=h: v.tensor_copy(
                            sb_ak[:, h * 10:(h + 1) * 10], ps[5][:, 0:10]),
                            wait=[tad], inc=("dv", 1))
                        ktok["ad_ev"] = tae
                    else:
                        tae = E("vector", lambda v: v.tensor_copy(
                            sb_ave[:, 0:10], ps[5][:, 0:10]),
                            wait=[tad, ktok.get("av_tp")], inc=("dv", 1))
                        tat = E("tensor", lambda t: t.transpose(
                            pstp[0][0:10, 896:1024], sb_ave[:, 0:10], sb_id[:]),
                            wait=[tae, pstp_free.get(0)], inc=("pe", 1))
                        ktok["av_tp"] = tat
                        tac = E("vector", lambda v: v.tensor_copy(
                            sb_avr[:], pstp[0][0:10, 896:1024]),
                            wait=[tat, ktok.get("avr_w")], inc=("dv", 1))
                        pstp_free[0] = tac
                        taw = E("sync", lambda g, h=h: g.dma_start(
                            out=d_avs[:, h * 128:(h + 1) * 128], in_=sb_avr[:]),
                            wait=[tac], inc=("savw", 16))
                        ktok["avr_w"] = taw
                        ktok["ad_ev"] = taw
        bar = P.barrier()

        gat_tok = {}
        for kap in range(8):
            par = kap % 2
            tga = E("sync", lambda g, kap=kap, par=par: g.dma_start(
                out=v_gA[par][:], in_=d_xn[kap * 128:(kap + 1) * 128, :]),
                wait=[bar, gat_tok.get(("a", par))], inc=("sga%d" % par, 16))
            tgb = E("sync", lambda g, kap=kap, par=par: g.dma_start(
                out=v_gath[par][:], in_=d_xn[1024 + kap * 128:1024 + (kap + 1) * 128, :]),
                wait=[bar, gat_tok.get(par)], inc=("sgb%d" % par, 16))
            tb1 = E("vector", lambda v, par=par: v.tensor_scalar(
                v_gath[par][:], v_gath[par][:], sb_sel[:, 1:2], None, ALU.mult),
                wait=[tgb, t_gx, gat_tok.get(par)], inc=("dv", 1))
            tg_ = E("vector", lambda v, par=par: v.scalar_tensor_tensor(
                v_gath[par][:], v_gA[par][:], sb_sel[:, 0:1], v_gath[par][:],
                ALU.mult, ALU.add),
                wait=[tb1, tga], inc=("dv", 1))
            gat_tok[("a", par)] = tg_
            for pk in range(4):
                pb = pk % 2
                tps = None
                for j in range(8):
                    cc = pk * 8 + j
                    tps = E("tensor", lambda t, cc=cc, pb=pb, j=j, par=par: t.transpose(
                        pstp[pb][:, j * 128:(j + 1) * 128],
                        v_gath[par][:, cc * 128:(cc + 1) * 128], sb_id[:]),
                        wait=([bar, tg_, pstp_free.get(pb)] if j == 0 else []),
                        inc=(("pe", 1) if j == 7 else None))
                tev = E("vector", lambda v, pk=pk, pb=pb, kap=kap: v.tensor_copy(
                    v_xno[:, pk * 8:(pk + 1) * 8, kap * 128:(kap + 1) * 128],
                    pstp[pb][:].rearrange("p (a b) -> p a b", a=8, b=128)),
                    wait=[bar, tps], inc=("dv", 1))
                pstp_free[pb] = tev
                gat_tok[par] = tev
        bar = P.barrier()

        for ob in range(32):
            h = ob
            par = ob % 2
            tsl = E("gpsimd", lambda g, ob=ob, par=par: g.dma_start(
                out=v_slab[par][:], in_=d_Aat[ob]),
                wait=[bar, slab_rd.get(("D", par))], inc=("gsl%d" % par, 16))
            for g2 in range(2):
                bank = (ob * 2 + g2) % 4
                tmm = None
                for cc in range(NCC):
                    tmm = E("tensor", lambda t, cc=cc, g2=g2, par=par, bank=bank: t.matmul(
                        ps[bank][:], v_slab[par][:, cc * 128:(cc + 1) * 128],
                        v_xno[:, cc, g2 * 512:(g2 + 1) * 512],
                        start=(cc == 0), stop=(cc == NCC - 1)),
                        wait=([bar, tsl, ps_free.get(bank)] if cc == 0 else []),
                        inc=(("pe", 1) if cc == NCC - 1 else None))
                stg = (ob * 2 + g2) % 4
                lcol = slice(g2 * 512, (g2 + 1) * 512)
                tvs = E("vector", lambda v, bank=bank, lcol=lcol: v.tensor_mul(
                    sb_vs[:], ps[bank][:], v_sinq[:, lcol]),
                    wait=[bar, tmm, t_cos, ktok.get("rope_mm")], inc=("dv", 1))
                trm = E("tensor", lambda t: t.matmul(
                    ps[4][:], sb_p64[:], sb_vs[:], start=True, stop=True),
                    wait=[tvs, ktok.get("rope_ev")], inc=("pe", 1))
                ktok["rope_mm"] = trm
                ttc = E("vector", lambda v, bank=bank, lcol=lcol: v.tensor_mul(
                    sb_tc[:], ps[bank][:], v_cosq[:, lcol]),
                    wait=[tmm, ktok.get("rope_ev")], inc=("dv", 1))
                ps_free[bank] = ttc
                tkr = E("vector", lambda v, stg=stg: v.tensor_add(
                    sb_kst[:, stg * 512:(stg + 1) * 512], sb_tc[:], ps[4][:]),
                    wait=[trm, ttc, ktok.get(("kst", stg))], inc=("dv", 1))
                ktok["rope_ev"] = tkr
                tst_w = E("sync", lambda g, h=h, lcol=lcol, stg=stg: g.dma_start(
                    out=d_qs[h][:, lcol], in_=sb_kst[:, stg * 512:(stg + 1) * 512]),
                    wait=[tkr], inc=("skst%d" % stg, 16))
                ktok[("kst", stg)] = tst_w
            slab_rd[("D", par)] = tmm
        bar = P.barrier()

        tXd_prev = None
        tXd_all = []
        for g2 in range(2):
            tmm = None
            for cc in range(NCC):
                tmm = E("tensor", lambda t, cc=cc, g2=g2: t.matmul(
                    ps[5][0:32, :], sb_st[:, 85 - cc:85 - cc + 32],
                    v_xno[:, cc, g2 * 512:(g2 + 1) * 512],
                    start=(cc == 0), stop=(cc == NCC - 1)),
                    wait=([bar, t_st, tXd_prev] if cc == 0 else []),
                    inc=(("pe", 1) if cc == NCC - 1 else None))
            tXd_prev = E("vector", lambda v, g2=g2: v.tensor_copy(
                sb_Xd[:, g2 * 512:(g2 + 1) * 512], ps[5][0:32, :]),
                wait=[bar, tmm], inc=("dv", 1))
            tXd_all.append(tXd_prev)

        dq_tok = {}
        hw_tok = {}
        for hb in range(NHB):
            par = hb % 2
            tla = E("gpsimd", lambda g, hb=hb, par=par: g.dma_start(
                out=v_wg[par][:], in_=d_Bg[hb]),
                wait=[bar, dq_tok.get(("wg", par))], inc=("gwg%d" % par, 16))
            tlb = E("gpsimd", lambda g, hb=hb, par=par: g.dma_start(
                out=v_sg[par][:], in_=d_sgb[hb:hb + 1, :].partition_broadcast(128)),
                wait=[dq_tok.get(("qa", par))], inc=("gsg%d" % par, 16))
            tlc = E("gpsimd", lambda g, hb=hb, par=par: g.dma_start(
                out=v_wu[par][:], in_=d_Bu[hb]),
                wait=[dq_tok.get(("wu", par))], inc=("gwu%d" % par, 16))
            tld = E("gpsimd", lambda g, hb=hb, par=par: g.dma_start(
                out=v_su[par][:], in_=d_sub[hb:hb + 1, :].partition_broadcast(128)),
                wait=[dq_tok.get(("qb", par))], inc=("gsu%d" % par, 16))
            tqa = E("vector", lambda v, par=par: v.tensor_mul(v_wg[par][:], v_wg[par][:],
                                                              v_sg[par][:]),
                    wait=[tla, tlb], inc=("dv", 1))
            tqb = E("gpsimd", lambda g, par=par: g.tensor_mul(v_wu[par][:], v_wu[par][:],
                                                              v_su[par][:]),
                    wait=[tlc, tld], inc=("gpc", 1))
            dq_tok[("qa", par)] = tqa
            dq_tok[("qb", par)] = tqb
            tmg = E("sync", lambda g, hb=hb, par=par: g.dma_start(
                out=sb_mtg[:, par * 128:(par + 1) * 128],
                in_=d_mtg[:, hb * 128:(hb + 1) * 128]),
                wait=[dq_tok.get(("mtg", par))], inc=("smtg%d" % par, 16))
            tmu = E("sync", lambda g, hb=hb, par=par: g.dma_start(
                out=sb_mtu[:, par * 128:(par + 1) * 128],
                in_=d_mtu[:, hb * 128:(hb + 1) * 128]),
                wait=[dq_tok.get(("mtu", par))], inc=("smtu%d" % par, 16))
            for g2 in range(2):
                bank_g = (hb * 2 + g2) % 2
                bank_u = 2 + (hb * 2 + g2) % 2
                for cc in range(NCC):
                    E("tensor", lambda t, cc=cc, g2=g2, par=par, bank=bank_g: t.matmul(
                        ps[bank][:], v_wg[par][:, cc * 128:(cc + 1) * 128],
                        v_xno[:, cc, g2 * 512:(g2 + 1) * 512],
                        start=(cc == 0), stop=False),
                        wait=([tqa, ps_free.get(bank_g)] + (tXd_all if hb == 0 else [])
                              if cc == 0 else []))
                tg_mm = E("tensor", lambda t, g2=g2, par=par, bank=bank_g: t.matmul(
                    ps[bank][:], sb_mtg[0:32, par * 128:(par + 1) * 128],
                    sb_Xd[:, g2 * 512:(g2 + 1) * 512], start=False, stop=True),
                    wait=[tmg], inc=("pe", 1))
                for cc in range(NCC):
                    E("tensor", lambda t, cc=cc, g2=g2, par=par, bank=bank_u: t.matmul(
                        ps[bank][:], v_wu[par][:, cc * 128:(cc + 1) * 128],
                        v_xno[:, cc, g2 * 512:(g2 + 1) * 512],
                        start=(cc == 0), stop=False),
                        wait=([tqb, ps_free.get(bank_u)] if cc == 0 else []))
                tu_mm = E("tensor", lambda t, g2=g2, par=par, bank=bank_u: t.matmul(
                    ps[bank][:], sb_mtu[0:32, par * 128:(par + 1) * 128],
                    sb_Xd[:, g2 * 512:(g2 + 1) * 512], start=False, stop=True),
                    wait=[tmu], inc=("pe", 1))
                dq_tok[("mtg", par)] = tg_mm
                dq_tok[("mtu", par)] = tu_mm
                tsi = E("scalar", lambda s, bank=bank_g, par=par: s.activation(
                    v_gsil[par][:], ps[bank][:], AF.Silu),
                    wait=[tg_mm, hw_tok.get(("gsil", par))], inc=("se", 1))
                ps_free[bank_g] = tsi
                th = E("vector", lambda v, g2=g2, bank=bank_u, par=par: v.tensor_mul(
                    v_hst[par][:, g2 * 512:(g2 + 1) * 512], v_gsil[par][:], ps[bank][:]),
                    wait=[tsi, tu_mm, hw_tok.get(("hst", par))], inc=("dv", 1))
                ps_free[bank_u] = th
                hw_tok[("gsil", par)] = th
                hw_tok[("h", g2, par)] = th
            thw = E("sync", lambda g, hb=hb, par=par: g.dma_start(
                out=d_hs[hb], in_=v_hst[par][:]),
                wait=[hw_tok[("h", 0, par)], hw_tok[("h", 1, par)]], inc=("shw%d" % par, 16))
            hw_tok[("hst", par)] = thw
            dq_tok[("wg", par)] = tg_mm
            dq_tok[("wu", par)] = tu_mm
        bar = P.barrier()

        e2 = {}
        for pss in range(2):
            th_l = None
            for hb in range(NHB):
                th_l = E("sync", lambda g, hb=hb, pss=pss: g.dma_start(
                    out=v_hhalf[:, hb, :], in_=d_hs[hb][:, pss * 512:(pss + 1) * 512]),
                    wait=([bar, e2.get("pass_done")] if hb == 0 else []), inc=("shh", 16))
            tmm = None
            for gh in range(NHB):
                tmm = E("tensor", lambda t, gh=gh: t.matmul(
                    ps[5][0:86, :], sb_st[:, 85 - gh:85 - gh + 86],
                    v_hhalf[:, gh, :], start=(gh == 0), stop=(gh == NHB - 1)),
                    wait=([bar, th_l, e2.get("Xh_ev")] if gh == 0 else []),
                    inc=(("pe", 1) if gh == NHB - 1 else None))
            tXh = E("vector", lambda v: v.tensor_copy(v_Xh[:], ps[5][0:86, :]),
                    wait=[bar, tmm], inc=("dv", 1))
            e2["Xh_ev"] = tXh
            tfw = None
            for db in range(NDB):
                par = db % 2
                tw = E("gpsimd", lambda g, db=db, par=par: g.dma_start(
                    out=v_wd[par].rearrange("p a b -> p (a b)"), in_=d_Bd[db]),
                    wait=[bar, e2.get(("wd", par))], inc=("gwd%d" % par, 16))
                tq = []
                for qh in range(2):
                    tsb = E("gpsimd", lambda g, db=db, qh=qh: g.dma_start(
                        out=v_sdq[:],
                        in_=d_sdb[db:db + 1, qh * 5504:(qh + 1) * 5504]
                        .partition_broadcast(128)),
                        wait=[e2.get("sdq")], inc=("gsdq", 16))
                    wv = v_wd[par].rearrange("p a b -> p (a b)")
                    tqh = E("vector", lambda v, wv=wv, qh=qh: v.tensor_mul(
                        wv[:, qh * 5504:(qh + 1) * 5504], wv[:, qh * 5504:(qh + 1) * 5504],
                        v_sdq[:]),
                        wait=[tw, tsb], inc=("dv", 1))
                    tq.append(tqh)
                    e2["sdq"] = tqh
                tmd = E("sync", lambda g, db=db, par=par: g.dma_start(
                    out=v_mtd_t[par][:], in_=d_mtd[:, db * 128:(db + 1) * 128]),
                    wait=[e2.get(("mtd", par))], inc=("smtd%d" % par, 16))
                bank = db % 2
                for gh in range(NHB):
                    E("tensor", lambda t, gh=gh, par=par, bank=bank: t.matmul(
                        ps[bank][:], v_wd[par][:, gh, :], v_hhalf[:, gh, :],
                        start=(gh == 0), stop=False),
                        wait=([tq[0], tXh, ps_free.get(("E2", bank))] if gh == 0
                              else ([tq[1]] if gh == 43 else [])))
                tdm = E("tensor", lambda t, par=par, bank=bank: t.matmul(
                    ps[bank][:], v_mtd_t[par][:], v_Xh[:], start=False, stop=True),
                    wait=[tmd], inc=("pe", 1))
                e2[("wd", par)] = tdm
                e2[("mtd", par)] = tdm
                tfe = E("vector", lambda v, bank=bank, par=par: v.tensor_copy(
                    v_fst[par][:], ps[bank][:]),
                    wait=[tdm, e2.get(("fst", par))], inc=("dv", 1))
                ps_free[("E2", bank)] = tfe
                tfw = E("sync", lambda g, db=db, pss=pss, par=par: g.dma_start(
                    out=d_fs[db][:, pss * 512:(pss + 1) * 512], in_=v_fst[par][:]),
                    wait=[tfe], inc=("sfw%d" % par, 16))
                e2[("fst", par)] = tfw
            e2["pass_done"] = tfw
        bar = P.barrier()

        tmask = E("sync", lambda g: g.dma_start(
            out=v_mask[:], in_=d_masks.rearrange("k c i j -> i (k c) j")),
            wait=[bar], inc=("smsk", 16))
        fa = {}
        for h in range(H):
            par = h % 2
            tkl = E("sync", lambda g, h=h, par=par: g.dma_start(
                out=v_kh[par][:], in_=d_ks[h]),
                wait=[bar, fa.get(("kh", par))], inc=("skh%d" % par, 16))
            tvl = E("sync", lambda g, h=h, par=par: g.dma_start(
                out=v_vh[par][:], in_=d_vs[h]),
                wait=[fa.get(("vh", par))], inc=("svh%d" % par, 16))
            tql = E("sync", lambda g, h=h, par=par: g.dma_start(
                out=v_qh[par][:], in_=d_qs[h]),
                wait=[fa.get(("qh", par))], inc=("sqh%d" % par, 16))
            tal = E("sync", lambda g, h=h, par=par: g.dma_start(
                out=sb_avt[:, par * 128:(par + 1) * 128],
                in_=d_avs[:, h * 128:(h + 1) * 128]),
                wait=[fa.get(("avt", par))], inc=("sat%d" % par, 16))
            for pk in range(2):
                pb = pk
                tps = None
                for j in range(8):
                    ct = pk * 8 + j
                    tps = E("tensor", lambda t, ct=ct, pb=pb, j=j, par=par: t.transpose(
                        pstp[pb][:, j * 128:(j + 1) * 128],
                        v_vh[par][:, ct * 128:(ct + 1) * 128], sb_id[:]),
                        wait=([bar, tvl, fa.get(("pstp", pb))] if j == 0 else []),
                        inc=(("pe", 1) if j == 7 else None))
                tev = E("scalar", lambda s, pk=pk, pb=pb: s.copy(
                    v_vtm[:, pk * 8:(pk + 1) * 8, :],
                    pstp[pb][:].rearrange("p (a b) -> p a b", a=8, b=128)),
                    wait=[tps, fa.get("ylast")], inc=("se", 1))
                fa[("pstp", pb)] = tev
                fa[("vtm", pk)] = tev
            fa[("vh", par)] = fa[("vtm", 1)]
            tam = None
            for tg in range(2):
                y_mm_wait = [fa.get(("vtm", 0)), fa.get(("vtm", 1))]
                for kl in range(4):
                    kap = tg * 4 + kl
                    lc = LCS[kap]
                    sc_toks = []
                    for cch in range(lc):
                        bank = cch % 2
                        tsm = E("tensor", lambda t, kap=kap, cch=cch, par=par, bank=bank: t.matmul(
                            ps[bank][:], v_qh[par][:, kap * 128:(kap + 1) * 128],
                            v_kh[par][:, cch * 512:(cch + 1) * 512], start=True, stop=True),
                            wait=[tkl, tql, fa.get(("Sb", bank))], inc=("pe", 1))
                        tse = E("vector", lambda v, kap=kap, cch=cch, bank=bank:
                                v.scalar_tensor_tensor(
                                    v_Ssb[:, cch * 512:(cch + 1) * 512], ps[bank][:], SCALE_S,
                                    v_mask[:, kap * 4 + cch, :], ALU.mult, ALU.add),
                                wait=[tsm, tmask, fa.get("exp_prev")], inc=("dv", 1))
                        fa[("Sb", bank)] = tse
                        sc_toks.append(tse)
                    tam = E("tensor", lambda t, h=h, kap=kap, par=par: t.matmul(
                        ps[5][:, 0:10], v_qh[par][:, kap * 128:(kap + 1) * 128],
                        sb_ak[:, h * 10:(h + 1) * 10], start=True, stop=True),
                        wait=[fa.get("asc_ev")], inc=("pe", 1))
                    tas = E("vector", lambda v: v.tensor_scalar(
                        v_asc[:, 0:10], ps[5][:, 0:10], SCALE_A, None, ALU.mult),
                        wait=[tam], inc=("dv", 1))
                    trm = E("vector", lambda v, lc=lc: v.reduce_max(
                        sb_cols[:, 52:53], v_Ssb[:, 0:lc * 512], axis=AX.X),
                        wait=sc_toks, inc=("dv", 1))
                    tng = E("vector", lambda v: v.tensor_scalar(
                        sb_cols[:, 53:54], sb_cols[:, 52:53], -1.0, None, ALU.mult),
                        wait=[trm], inc=("dv", 1))
                    tex = E("scalar", lambda s, lc=lc: s.activation(
                        v_Pb[:, 0:lc * 512], v_Ssb[:, 0:lc * 512], AF.Exp,
                        bias=sb_cols[:, 53:54], accum_out=sb_cols[:, 54:55]),
                        wait=[tng, fa.get("ptp_last")], inc=("se", 1))
                    fa["exp_prev"] = tex
                    trc = E("vector", lambda v: v.reciprocal(sb_cols[:, 55:56],
                                                             sb_cols[:, 54:55]),
                            wait=[tex], inc=("dv", 1))
                    tpn = E("vector", lambda v, lc=lc: v.tensor_scalar(
                        v_Pb[:, 0:lc * 512], v_Pb[:, 0:lc * 512], sb_cols[:, 55:56], None,
                        ALU.mult), wait=[trc], inc=("dv", 1))
                    tram = E("vector", lambda v: v.reduce_max(
                        sb_cols[:, 56:57], v_asc[:, 0:10], axis=AX.X),
                        wait=[tas], inc=("dv", 1))
                    trng = E("vector", lambda v: v.tensor_scalar(
                        sb_cols[:, 57:58], sb_cols[:, 56:57], -1.0, None, ALU.mult),
                        wait=[tram], inc=("dv", 1))
                    texa = E("scalar", lambda s: s.activation(
                        v_Pa[:, 0:10], v_asc[:, 0:10], AF.Exp,
                        bias=sb_cols[:, 57:58], accum_out=sb_cols[:, 58:59]),
                        wait=[trng, fa.get(("pstp", 0))], inc=("se", 1))
                    fa["asc_ev"] = texa
                    trca = E("vector", lambda v: v.reciprocal(sb_cols[:, 59:60],
                                                              sb_cols[:, 58:59]),
                             wait=[texa], inc=("dv", 1))
                    trg = E("vector", lambda v: v.tensor_scalar(
                        sb_cols[:, 60:61], sb_cols[:, 59:60], sb_gat[:], None, ALU.mult),
                        wait=[trca, t_gt], inc=("dv", 1))
                    tpa = E("vector", lambda v: v.tensor_scalar(
                        v_Pa[:, 0:10], v_Pa[:, 0:10], sb_cols[:, 60:61], None, ALU.mult),
                        wait=[trg], inc=("dv", 1))
                    nct = lc * 4
                    tps = None
                    for pk in range((nct + 7) // 8):
                        pb = pk % 2
                        j0 = pk * 8
                        jn = min(8, nct - j0)
                        for j in range(jn):
                            ct = j0 + j
                            tps = E("tensor", lambda t, ct=ct, pb=pb, j=j: t.transpose(
                                pstp[pb][:, j * 128:(j + 1) * 128],
                                v_Pb[:, ct * 128:(ct + 1) * 128], sb_id[:]),
                                wait=([tpn, fa.get(("pstp", pb))] if j == 0 else []),
                                inc=(("pe", 1) if j == jn - 1 else None))
                        tev = E("scalar", lambda s, j0=j0, jn=jn, pb=pb, kl=kl: s.copy(
                            v_PT[:, j0:j0 + jn, kl * 128:(kl + 1) * 128],
                            pstp[pb][:, 0:jn * 128].rearrange("p (a b) -> p a b", a=jn, b=128)),
                            wait=[tps, fa.get("ylast")], inc=("se", 1))
                        fa[("pstp", pb)] = tev
                        y_mm_wait.append(tev)
                    fa["ptp_last"] = tps
                    tpt = E("tensor", lambda t: t.transpose(
                        pstp[0][0:10, 896:1024], v_Pa[:, 0:10], sb_id[:]),
                        wait=[tpa, fa.get(("pstp", 0))], inc=("pe", 1))
                    tpe = E("scalar", lambda s, kl=kl: s.copy(
                        sb_pan[:, kl * 128:(kl + 1) * 128], pstp[0][0:10, 896:1024]),
                        wait=[tpt, fa.get("ylast")], inc=("se", 1))
                    fa[("pstp", 0)] = tpe
                    y_mm_wait.append(tpe)
                nct_tg = 12 if tg == 0 else 16
                bank = 2 + tg % 2
                for ct in range(nct_tg):
                    E("tensor", lambda t, ct=ct, bank=bank: t.matmul(
                        ps[bank][:], v_vtm[:, ct, :], v_PT[:, ct, :],
                        start=(ct == 0), stop=False),
                        wait=(y_mm_wait + [fa.get(("yb", bank))] if ct == 0 else []))
                tym = E("tensor", lambda t, bank=bank, par=par: t.matmul(
                    ps[bank][:], sb_avt[:, par * 128:(par + 1) * 128],
                    sb_pan[:], start=False, stop=True),
                    wait=[tal], inc=("pe", 1))
                fa["ylast"] = tym
                fa[("avt", par)] = tym
                fa[("kh", par)] = tym
                fa[("qh", par)] = tym
                tye = E("vector", lambda v, h=h, tg=tg, bank=bank: v.tensor_copy(
                    v_y[:, h, tg * 512:(tg + 1) * 512], ps[bank][:]),
                    wait=[tym], inc=("dv", 1))
                fa[("yb", bank)] = tye
        bar = P.barrier()

        gtok = {}
        so_count = 0
        for ob in range(NDB):
            par = ob % 2
            tsl = E("gpsimd", lambda g, ob=ob, par=par: g.dma_start(
                out=v_slab[par][:], in_=d_Apr[ob]),
                wait=[bar, gtok.get(("sl", par))], inc=("gsl%d" % par, 16))
            tfl = E("sync", lambda g, ob=ob, par=par: g.dma_start(
                out=v_ft[par][:], in_=d_fs[ob]),
                wait=[bar, gtok.get(("f", par))], inc=("sfl%d" % par, 16))
            txl = E("sync", lambda g, ob=ob, par=par: g.dma_start(
                out=v_xt[par][:], in_=d_xT[ob]),
                wait=[gtok.get(("x", par))], inc=("sxt%d" % par, 16))
            for tg in range(2):
                bank = (ob * 2 + tg) % 4
                tmm = None
                for cc in range(NCC):
                    tmm = E("tensor", lambda t, cc=cc, tg=tg, par=par, bank=bank: t.matmul(
                        ps[bank][:], v_slab[par][:, cc * 128:(cc + 1) * 128],
                        v_y[:, cc, tg * 512:(tg + 1) * 512],
                        start=(cc == 0), stop=(cc == NCC - 1)),
                        wait=([bar, tsl, gtok.get(("ps", bank))] if cc == 0 else []),
                        inc=(("pe", 1) if cc == NCC - 1 else None))
                gtok[("sl", par)] = tmm
                t1 = E("vector", lambda v, tg=tg, par=par, bank=bank: v.scalar_tensor_tensor(
                    v_ost[par][:, tg * 512:(tg + 1) * 512], ps[bank][:], 1.0,
                    v_ft[par][:, tg * 512:(tg + 1) * 512], ALU.mult, ALU.add),
                    wait=[tmm, tfl, gtok.get(("ost", par))], inc=("dv", 1))
                gtok[("ps", bank)] = t1
                t2 = E("vector", lambda v, tg=tg, par=par: v.tensor_add(
                    v_ost[par][:, tg * 512:(tg + 1) * 512],
                    v_ost[par][:, tg * 512:(tg + 1) * 512],
                    v_xt[par][:, tg * 512:(tg + 1) * 512]),
                    wait=[txl, t1], inc=("dv", 1))
                gtok[("ev", tg, par)] = t2
            gtok[("f", par)] = gtok[("ev", 1, par)]
            gtok[("x", par)] = gtok[("ev", 1, par)]
            tow = E("sync", lambda g, ob=ob, par=par: g.dma_start(
                out=d_out[ob], in_=v_ost[par][:]),
                wait=[gtok[("ev", 0, par)], gtok[("ev", 1, par)]], inc=("sout%d" % par, 16))
            so_count += 16
            gtok[("ost", par)] = tow
        out_done = [("sout0", P.val["sout0"]), ("sout1", P.val["sout1"])]
        E("sync", lambda g: None, wait=out_done)

        if _DEBUG_TAPS:
            tapbar = P.barrier()
            for nm in _DEBUG_TAPS:
                ap, shape = scratch_reg[nm]
                o = dt_("tap_" + nm, shape, F32, kind="ExternalOutput").ap()
                E("gpsimd", lambda g, o=o, ap=ap: g.dma_start(out=o, in_=ap),
                  wait=tapbar, inc=("gtap", 16))
            E("gpsimd", lambda g: None, wait=[("gtap", P.val["gtap"])])

        if _FIN_TAIL:
            endbar = P.barrier()
            P.val["fin"] = 0
            t_fmz = E("vector", lambda v: v.memset(sb_cols[:, 62:63], 0.0), wait=endbar,
                      inc=("fin2", 1))
            E("scalar", lambda s: s.copy(sb_cols[:, 61:62], sb_cols[:, 63:64]),
              wait=endbar, inc=("fin", 1))
            E("tensor", lambda t: t.ldweights(sb_id[:]), wait=endbar, inc=("fin", 1))
            d_dummy = dt_("fin_dummy", [1, 1], F32).ap()
            E("sync", lambda g: g.dma_start(out=d_dummy, in_=sb_cols[0:1, 62:63]),
              wait=endbar + [t_fmz], inc=("fin", 16))
            def _cleanup(g):
                if _CLEANUP < 2:
                    return None
                nums = sorted(sems[s].num for s in P.val)
                rng = range(min(nums), max(nums) + 1)
                g.dma_reset(rng)
                return g.sem_clear(rng)
            E("gpsimd", _cleanup, wait=endbar + [("fin", 18), ("fin2", 1)])

        sems = {name: en(nc.semaphore(name)) for name in sorted(P.val)}
        P.run(block, sems)
        if _CLEANUP >= 2:
            nc.all_engine_barrier()
            nc.clear_and_free_semaphores([sems[s] for s in sorted(P.val)])
            nc.all_engine_barrier()
    return nc


def _host_prep(inputs):
    x = np.asarray(inputs["L0_attn_x"], np.float32)
    w_attn = np.asarray(inputs["L0_attn_w_attn"], np.float32)
    w_proj = np.asarray(inputs["L0_attn_w_proj"], np.float32)
    adapter = np.asarray(inputs["L0_attn_adapter_wte"], np.float32)
    gating = np.ascontiguousarray(np.asarray(inputs["L0_attn_gating"],
                                             np.float32).reshape(1, 1))
    qw_g = np.asarray(inputs["L0_ffn_qw_gate"], np.int32)
    z_g = np.asarray(inputs["L0_ffn_z_gate"], np.int32)
    s_g = np.asarray(inputs["L0_ffn_s_gate"], np.float32)
    qw_u = np.asarray(inputs["L0_ffn_qw_up"], np.int32)
    z_u = np.asarray(inputs["L0_ffn_z_up"], np.int32)
    s_u = np.asarray(inputs["L0_ffn_s_up"], np.float32)
    qw_d = np.asarray(inputs["L0_ffn_qw_down"], np.int32)
    z_d = np.asarray(inputs["L0_ffn_z_down"], np.int32)
    s_d = np.asarray(inputs["L0_ffn_s_down"], np.float32)

    com = {}
    com["A_attn"] = np.ascontiguousarray(
        w_attn.reshape(96, 128, 32, 128).transpose(0, 3, 2, 1)).reshape(96, 128, 4096)
    com["A_proj"] = np.ascontiguousarray(
        w_proj.reshape(32, 128, 32, 128).transpose(0, 3, 2, 1)).reshape(32, 128, 4096)
    com["A_ad"] = np.ascontiguousarray(
        adapter.reshape(10, 32, 128).transpose(2, 1, 0)).reshape(128, 320)
    com["B_g"] = np.ascontiguousarray(
        qw_g.reshape(32, 128, 86, 128).transpose(2, 1, 0, 3)).reshape(86, 128, 4096)
    com["B_u"] = np.ascontiguousarray(
        qw_u.reshape(32, 128, 86, 128).transpose(2, 1, 0, 3)).reshape(86, 128, 4096)
    com["B_d"] = np.ascontiguousarray(
        qw_d.reshape(86, 128, 32, 128).transpose(2, 1, 0, 3)).reshape(32, 128, 11008)
    com["s_g_t"] = np.ascontiguousarray(
        s_g.reshape(32, 86, 128).transpose(1, 0, 2)).reshape(128, 2752)
    com["s_u_t"] = np.ascontiguousarray(
        s_u.reshape(32, 86, 128).transpose(1, 0, 2)).reshape(128, 2752)
    com["s_d_t"] = np.ascontiguousarray(
        s_d.reshape(86, 32, 128).transpose(1, 0, 2)).reshape(128, 2752)
    com["s_g_n"] = np.ascontiguousarray(s_g).reshape(128, 2752)
    com["s_u_n"] = np.ascontiguousarray(s_u).reshape(128, 2752)
    com["s_d_n"] = np.ascontiguousarray(s_d).reshape(128, 2752)
    com["z_g_n"] = np.ascontiguousarray(z_g).reshape(128, 2752)
    com["z_u_n"] = np.ascontiguousarray(z_u).reshape(128, 2752)
    com["z_d_n"] = np.ascontiguousarray(z_d).reshape(128, 2752)
    com["gating"] = gating

    j = np.arange(64, dtype=np.float64)
    theta = 1.0 / (10000.0 ** (j / 64.0))
    tt = np.arange(T, dtype=np.float64)
    ang = np.outer(theta, tt)
    cos_f = np.cos(ang)
    sin_f = np.sin(ang)
    com["cos_f"] = np.ascontiguousarray(np.concatenate([cos_f, cos_f], 0)).astype(np.float32)
    com["sin_f"] = np.ascontiguousarray(np.concatenate([sin_f, sin_f], 0)).astype(np.float32)

    maps = []
    for c in range(8):
        b, p = c // 2, c % 2
        m = dict(com)
        m["x_seq"] = np.ascontiguousarray(x[b])
        m["xT_own"] = np.ascontiguousarray(
            x[b, p * 1024:(p + 1) * 1024, :].T).reshape(32, 128, 1024)
        m["cos_q"] = np.ascontiguousarray(com["cos_f"][:, p * 1024:(p + 1) * 1024])
        m["sin_q"] = np.ascontiguousarray(com["sin_f"][:, p * 1024:(p + 1) * 1024])
        m["sel"] = np.ascontiguousarray(
            np.tile(np.array([[1.0 - p, float(p)]], np.float32), (128, 1)))
        msk = np.zeros((8, 4, 128, 512), np.float32)
        for kap in range(8):
            a = p * 8 + kap
            qg = a * 128 + np.arange(128)[:, None]
            for ch in range(4):
                kg = ch * 512 + np.arange(512)[None, :]
                msk[kap, ch] = np.where(kg <= qg, 0.0, NEG)
        m["masks"] = np.ascontiguousarray(msk.astype(BF16NP)).view(np.uint16)
        maps.append(m)
    return maps


_NC_CACHE = {}


def kernel(**inputs):
    maps = _host_prep(inputs)
    if "nc" not in _NC_CACHE:
        _NC_CACHE["nc"] = _build()
    nc = _NC_CACHE["nc"]
    res = run_bass_kernel_spmd(nc, maps, core_ids=list(range(8)))
    out = np.empty((B, T, C), np.float32)
    for c in range(8):
        b, p = c // 2, c % 2
        o = np.asarray(res.results[c]["out_fm"]).reshape(4096, 1024)
        out[b, p * 1024:(p + 1) * 1024, :] = o.T
    return out
